# version 01 (final)
# baseline (speedup 1.0000x reference)
import numpy as np
from contextlib import ExitStack

import concourse.bass as bass
import concourse.mybir as mybir
from concourse.bass_utils import run_bass_kernel_spmd

F32 = mybir.dt.float32
BF16 = mybir.dt.bfloat16
AF = mybir.ActivationFunctionType
ALU = mybir.AluOpType

ENGS = ("sp", "pe", "act", "dve", "pool")
D = 2048
S = 4096
NCH = 16
NTQ = 8
TQW = 512
HD = 128
NH = 16
FF = 8192
ISQ = 1.0 / np.sqrt(128.0)


class Buf:
    __slots__ = ("w", "r")

    def __init__(self):
        self.w = None
        self.r = {}


class Prog:
    def __init__(self):
        self.streams = {e: [] for e in ENGS}
        self.cnt = dict.fromkeys(ENGS, 0)
        self.hw = {e: dict.fromkeys(ENGS, 0) for e in ENGS}

    def _need(self, deps, tok):
        if tok is None:
            return
        e, v = tok
        if deps.get(e, 0) < v:
            deps[e] = v

    def emit(self, eng, fn, reads=(), writes=(), inc=1):
        deps = {}
        for b in reads:
            self._need(deps, b.w)
        for b in writes:
            self._need(deps, b.w)
            for e, v in b.r.items():
                self._need(deps, (e, v))
        for e2 in ENGS:
            v = deps.get(e2, 0)
            if v <= self.hw[eng][e2]:
                continue
            if e2 == eng and eng in ("pe", "sp", "pool"):
                self.hw[eng][e2] = v
                continue
            self.streams[eng].append(("w", e2, v))
            self.hw[eng][e2] = v
        self.cnt[eng] += inc
        tok = (eng, self.cnt[eng])
        self.streams[eng].append(("o", fn, inc))
        for b in reads:
            if b.r.get(eng, 0) < tok[1]:
                b.r[eng] = tok[1]
        for b in writes:
            b.w = tok
            b.r = {}
        return tok

    def barrier(self):
        snap = dict(self.cnt)
        for e in ENGS:
            for e2 in ENGS:
                if e2 != e and snap[e2] > self.hw[e][e2]:
                    self.streams[e].append(("w", e2, snap[e2]))
                    self.hw[e][e2] = snap[e2]


def build_program():
    nc = bass.Bass("TRN2", target_bir_lowering=False, debug=False)
    p = Prog()

    EI = dict(kind="ExternalInput")
    IN = dict(kind="Internal")

    dr = {}
    def dram(name, shape, dt, kind):
        dr[name] = nc.dram_tensor(name, shape, dt, kind=kind).ap()
        return dr[name]

    xt = dram("xt", [D, S], F32, "ExternalInput")
    wqkvT = dram("wqkvT", [D, 3 * D], F32, "ExternalInput")
    waoT = dram("waoT", [D, D], F32, "ExternalInput")
    fwqT = dram("fwqT", [D, D], F32, "ExternalInput")
    fwkT = dram("fwkT", [D, D], F32, "ExternalInput")
    fwvT = dram("fwvT", [D, D], F32, "ExternalInput")
    fwoT = dram("fwoT", [D, D], F32, "ExternalInput")
    fA = dram("fA", [D, 16], F32, "ExternalInput")
    fB = dram("fB", [16, FF], F32, "ExternalInput")
    fwinT = dram("fwinT", [D, FF], F32, "ExternalInput")
    fwupT = dram("fwupT", [D, FF], F32, "ExternalInput")
    fwdT = dram("fwdT", [FF, D], F32, "ExternalInput")
    bwqT = dram("bwqT", [D, D], F32, "ExternalInput")
    bwkT = dram("bwkT", [D, 512], F32, "ExternalInput")
    bwvT = dram("bwvT", [D, 512], F32, "ExternalInput")
    bwoT = dram("bwoT", [D, D], F32, "ExternalInput")
    ln1 = dram("ln1", [128, 16], F32, "ExternalInput")
    ln2 = dram("ln2", [128, 16], F32, "ExternalInput")
    ropecd = dram("ropec", [128, S], F32, "ExternalInput")
    ropesd = dram("ropes", [128, S], F32, "ExternalInput")
    dmaskd = dram("dmask", [128, 4 * TQW], F32, "ExternalInput")
    trild = dram("tril", [128, 128], F32, "ExternalInput")
    idend = dram("iden", [128, 128], F32, "ExternalInput")
    onescold = dram("onescol", [128, 1], F32, "ExternalInput")
    onesrowd = dram("onesrow", [1, 128], F32, "ExternalInput")

    R = dram("R", [D, S], F32, "Internal")
    XN = dram("XN", [D, S], F32, "Internal")
    XNB = dram("XNB", [D, S], BF16, "Internal")
    QB = dram("QB", [D, S], BF16, "Internal")
    KB = dram("KB", [D, S], BF16, "Internal")
    VB = dram("VB", [D, S], BF16, "Internal")
    AOB = dram("AOB", [D, S], BF16, "Internal")
    ATTB = dram("ATTB", [D, S], BF16, "Internal")
    X1B = dram("X1B", [D, S], BF16, "Internal")
    HB2 = dram("HB2", [D, S], BF16, "Internal")
    PRD = dram("PRD", [FF, S], BF16, "Internal")
    QF2 = dram("QF2", [D, S], BF16, "Internal")
    KF2 = dram("KF2", [512, S], BF16, "Internal")
    VB2 = dram("VB2", [512, S], BF16, "Internal")
    OUT = dram("out", [D, S], F32, "ExternalOutput")

    dbuf = {}
    def B(name):
        if name not in dbuf:
            dbuf[name] = Buf()
        return dbuf[name]

    def rtiles(name, dc, c0, w):
        out = []
        for cc in range(c0 // 128, (c0 + w) // 128):
            out.append(B((name, dc, cc)))
        return out

    with ExitStack() as stk:
        sems = {e: stk.enter_context(nc.semaphore("sem_" + e)) for e in ENGS}

        ARB = stk.enter_context(nc.sbuf_tensor("arb", [128, 72192], BF16))
        ARF = stk.enter_context(nc.sbuf_tensor("arf", [128, 12544], F32))
        PS = [stk.enter_context(nc.psum_tensor("ps%d" % i, [128, 512], F32))
              for i in range(6)]
        PSB = stk.enter_context(nc.psum_tensor("psb0", [128, 1024], BF16))
        PSB2 = stk.enter_context(nc.psum_tensor("psb1", [128, 1024], BF16))
        psbuf = [Buf() for _ in range(6)]
        psbbuf = [Buf(), Buf()]

        IDB = ARB[:, 0:128]
        ONESB = ARB[:, 128:129]
        DM = ARB[:, 144:144 + 4 * TQW]
        TRILB = ARB[:, 2192:2320]
        RCOS = ARB[:, 2320:2320 + S]
        RSIN = ARB[:, 6416:6416 + S]
        FAB = ARB[:, 10512:10512 + 256]
        FBB = ARB[0:16, 10768:10768 + FF]
        MHA = ARB[0:16, 18960:18960 + S]
        PB0 = 23040
        mha_buf = Buf()

        LNW1 = ARF[:, 0:16]
        LNW2 = ARF[:, 16:32]
        ONES1R = ARF[0:1, 32:160]
        ONESF = ARF[:, 160:161]
        PF0 = 192

        consts_buf = Buf()

        _ccells = {}

        def cc(val, pp=1):
            if val not in _ccells:
                col = 161 + len(_ccells)
                assert col < 176
                _ccells[val] = col
                p.emit("dve",
                       lambda E, a=ARF[:, col:col + 1], v=float(val): E.memset(a, v),
                       reads=[], writes=[consts_buf])
                nc.const_aps.register(F32, float(val), ARF[:, col:col + 1])
            return float(val)

        def LD(dst_ap, src_ap, reads=(), writes=()):
            return p.emit("sp", lambda E, o=dst_ap, i=src_ap: E.dma_start(out=o, in_=i),
                          reads=list(reads), writes=list(writes), inc=16)

        def ST(dst_ap, src_ap, reads=(), writes=()):
            return p.emit("pool", lambda E, o=dst_ap, i=src_ap: E.dma_start(out=o, in_=i),
                          reads=list(reads), writes=list(writes), inc=16)

        def LD3(dst2d, src_sl, jw, reads=(), writes=()):
            o = dst2d.rearrange("p (c j) -> p c j", j=jw)
            i = src_sl.rearrange("(c p) j -> p c j", p=128)
            return p.emit("sp", lambda E, oo=o, ii=i: E.dma_start(out=oo, in_=ii),
                          reads=list(reads), writes=list(writes), inc=16)

        def MM(out_ap, lhsT, rhs, start, stop, reads, wbuf):
            p.emit("pe",
                   lambda E, o=out_ap, a=lhsT, b=rhs, s=start, t=stop:
                   E.matmul(o, a, b, start=s, stop=t, skip_group_check=True),
                   reads=list(reads), writes=[wbuf])

        def TRANS(out_ap, in_ap, reads, wbuf):
            p.emit("pe",
                   lambda E, o=out_ap, i=in_ap: E.transpose(o, i, IDB),
                   reads=list(reads), writes=[wbuf])

        def ACT(out_ap, in_ap, func, reads, writes, bias=0.0, scale=1.0):
            if isinstance(bias, float) and bias != 0.0:
                bias = cc(bias)
            if isinstance(scale, float) and scale not in (1.0,):
                scale = cc(scale)
            p.emit("act",
                   lambda E, o=out_ap, i=in_ap, f=func, bb=bias, ss=scale:
                   E.activation(o, i, f, bias=bb, scale=ss),
                   reads=list(reads) + [consts_buf], writes=list(writes))

        def TT(out_ap, a_ap, b_ap, op, reads, writes):
            p.emit("dve",
                   lambda E, o=out_ap, a=a_ap, b=b_ap, oo=op:
                   E.tensor_tensor(o, a, b, oo),
                   reads=list(reads), writes=list(writes))

        def RECIP(out_ap, in_ap, reads, writes):
            p.emit("dve", lambda E, o=out_ap, i=in_ap: E.reciprocal(o, i),
                   reads=list(reads), writes=list(writes))

        stg = ARF[:, PF0:PF0 + S]
        stgb = Buf()
        def cvt_const(dram_ap, sb_dst, rows=128, cols=None):
            c = cols if cols is not None else dram_ap.shape[1]
            LD(stg[0:rows, 0:c], dram_ap[0:rows, 0:c], writes=[stgb])
            ACT(sb_dst, stg[0:rows, 0:c], AF.Copy, reads=[stgb], writes=[consts_buf])

        cvt_const(idend, IDB)
        cvt_const(onescold, ONESB)
        cvt_const(dmaskd, DM)
        cvt_const(trild, TRILB)
        cvt_const(ropecd, RCOS)
        cvt_const(ropesd, RSIN)
        for c in range(NCH):
            LD(stg[:, c * 16:(c + 1) * 16], fA[c * 128:(c + 1) * 128, :], writes=[stgb])
        ACT(FAB, stg[:, 0:256], AF.Copy, reads=[stgb], writes=[consts_buf])
        for hh in range(2):
            LD(stg[0:16, 0:4096], fB[0:16, hh * 4096:(hh + 1) * 4096], writes=[stgb])
            ACT(FBB[:, hh * 4096:(hh + 1) * 4096], stg[0:16, 0:4096], AF.Copy,
                reads=[stgb], writes=[consts_buf])
        LD(LNW1, ln1, writes=[consts_buf])
        LD(LNW2, ln2, writes=[consts_buf])
        LD(ONES1R, onesrowd, writes=[consts_buf])
        LD(ONESF, onescold, writes=[consts_buf])
        for v in (-1.0 / D, 1.0 / D, 1e-5, 1e-6, -1.0, 0.1, ISQ):
            cc(float(v), 1)
        p.barrier()

        def ln_phase(src_f32, src_keyed, dst_f32, dst_b16, copy_res):
            XT = ARF[:, PF0:PF0 + 8192]
            SQ = [ARF[:, PF0 + 8192 + i * 512:PF0 + 8192 + (i + 1) * 512] for i in range(2)]
            st_nm = ARF[0:1, PF0 + 9216:PF0 + 9216 + 512]
            st_m2 = ARF[0:1, PF0 + 9728:PF0 + 9728 + 512]
            st_v = ARF[0:1, PF0 + 10240:PF0 + 10240 + 512]
            st_rs = ARF[0:1, PF0 + 10752:PF0 + 10752 + 512]
            ot_f = ARF[:, PF0 + 11264:PF0 + 11264 + 512]
            xtb = [Buf() for _ in range(NCH)]
            sqb = [Buf(), Buf()]
            stb = [Buf() for _ in range(4)]
            otb = Buf()
            obb = Buf()
            OTB = ARB[:, PB0:PB0 + 512]
            for tq in range(NTQ):
                c0 = tq * TQW
                for c in range(NCH):
                    rd = rtiles("R", c, c0, TQW) if src_keyed else [B("XSRC")]
                    LD(XT[:, c * 512:(c + 1) * 512],
                       src_f32[c * 128:(c + 1) * 128, c0:c0 + TQW],
                       reads=rd, writes=[xtb[c]])
                    if copy_res:
                        ST(R[c * 128:(c + 1) * 128, c0:c0 + TQW],
                           XT[:, c * 512:(c + 1) * 512],
                           reads=[xtb[c]], writes=rtiles("R", c, c0, TQW))
                for c in range(NCH):
                    MM(PS[0][0:1, :], ONESF, XT[:, c * 512:(c + 1) * 512],
                       c == 0, c == NCH - 1, [xtb[c]], psbuf[0])
                    ACT(SQ[c % 2], XT[:, c * 512:(c + 1) * 512], AF.Square,
                        reads=[xtb[c]], writes=[sqb[c % 2]])
                    MM(PS[1][0:1, :], ONESF, SQ[c % 2],
                       c == 0, c == NCH - 1, [sqb[c % 2]], psbuf[1])
                ACT(st_nm, PS[0][0:1, :], AF.Copy, reads=[psbuf[0]], writes=[stb[0]],
                    scale=-1.0 / D)
                ACT(st_m2, st_nm, AF.Square, reads=[stb[0]], writes=[stb[1]])
                ACT(st_v, PS[1][0:1, :], AF.Copy, reads=[psbuf[1]], writes=[stb[2]],
                    scale=1.0 / D)
                TT(st_v, st_v, st_m2, ALU.subtract, reads=[stb[1], stb[2]], writes=[stb[2]])
                ACT(st_rs, st_v, AF.Sqrt, reads=[stb[2]], writes=[stb[3]], bias=1e-5)
                RECIP(st_rs, st_rs, reads=[stb[3]], writes=[stb[3]])
                MM(PS[2][:, :], ONES1R, st_nm, True, True, [stb[0]], psbuf[2])
                MM(PS[3][:, :], ONES1R, st_rs, True, True, [stb[3]], psbuf[3])
                for c in range(NCH):
                    TT(ot_f, XT[:, c * 512:(c + 1) * 512], PS[2][:, :], ALU.add,
                       reads=[xtb[c], psbuf[2]], writes=[otb])
                    TT(ot_f, ot_f, PS[3][:, :], ALU.mult,
                       reads=[otb, psbuf[3]], writes=[otb])
                    ST(dst_f32[c * 128:(c + 1) * 128, c0:c0 + TQW], ot_f,
                       reads=[otb], writes=[B("XN")])
                    ACT(OTB, ot_f, AF.Copy, reads=[otb], writes=[obb])
                    ST(dst_b16[c * 128:(c + 1) * 128, c0:c0 + TQW], OTB,
                       reads=[obb], writes=[B("XNB")])
            p.barrier()

        def rms_phase():
            XT = ARF[:, PF0:PF0 + 8192]
            SQ = [ARF[:, PF0 + 8192 + i * 512:PF0 + 8192 + (i + 1) * 512] for i in range(2)]
            st_rs = ARF[0:1, PF0 + 9216:PF0 + 9216 + 512]
            ot_f = ARF[:, PF0 + 9728:PF0 + 9728 + 512]
            xtb = [Buf() for _ in range(NCH)]
            sqb = [Buf(), Buf()]
            stb = Buf()
            otb = Buf()
            obb = [Buf(), Buf()]
            OTB1 = ARB[:, PB0:PB0 + 512]
            OTB2 = ARB[:, PB0 + 512:PB0 + 1024]
            for tq in range(NTQ):
                c0 = tq * TQW
                for c in range(NCH):
                    LD(XT[:, c * 512:(c + 1) * 512],
                       XN[c * 128:(c + 1) * 128, c0:c0 + TQW],
                       reads=[B("XN")], writes=[xtb[c]])
                for c in range(NCH):
                    ACT(SQ[c % 2], XT[:, c * 512:(c + 1) * 512], AF.Square,
                        reads=[xtb[c]], writes=[sqb[c % 2]])
                    MM(PS[1][0:1, :], ONESF, SQ[c % 2],
                       c == 0, c == NCH - 1, [sqb[c % 2]], psbuf[1])
                ACT(st_rs, PS[1][0:1, :], AF.Sqrt, reads=[psbuf[1]], writes=[stb],
                    scale=1.0 / D, bias=1e-6)
                RECIP(st_rs, st_rs, reads=[stb], writes=[stb])
                MM(PS[3][:, :], ONES1R, st_rs, True, True, [stb], psbuf[3])
                for c in range(NCH):
                    TT(ot_f, XT[:, c * 512:(c + 1) * 512], PS[3][:, :], ALU.mult,
                       reads=[xtb[c], psbuf[3]], writes=[otb])
                    p.emit("act",
                           lambda E, o=OTB1, i=ot_f, sc=LNW1[:, c:c + 1]:
                           E.activation(o, i, AF.Copy, scale=sc),
                           reads=[otb, consts_buf], writes=[obb[0]])
                    ST(X1B[c * 128:(c + 1) * 128, c0:c0 + TQW], OTB1,
                       reads=[obb[0]], writes=[B("X1B")])
                    p.emit("act",
                           lambda E, o=OTB2, i=ot_f, sc=LNW2[:, c:c + 1]:
                           E.activation(o, i, AF.Copy, scale=sc),
                           reads=[otb, consts_buf], writes=[obb[1]])
                    ST(HB2[c * 128:(c + 1) * 128, c0:c0 + TQW], OTB2,
                       reads=[obb[1]], writes=[B("HB2")])
            p.barrier()

        def proj(wT, n_oc, src_dram, src_name, outmap):
            OG = min(8, n_oc)
            WGB = ARB[:, PB0:PB0 + OG * 2048]
            SRCS = [ARB[:, PB0 + 16384 + i * 8192:PB0 + 16384 + (i + 1) * 8192]
                    for i in range(2)]
            SCR = PB0 + 32768
            wgb = [Buf() for _ in range(OG)]
            srcb = [Buf(), Buf()]
            WST = [ARF[:, PF0 + i * 2048:PF0 + (i + 1) * 2048] for i in range(2)]
            wstb = [Buf(), Buf()]
            tiles = {}
            tbuf = {}
            for par in range(2):
                base = SCR + par * 4096
                for nm_i, nm in enumerate(("TT", "RT", "O1", "O2", "OT", "R1", "DB", "EB")):
                    tiles[(nm, par)] = ARB[:, base + nm_i * 512:base + (nm_i + 1) * 512]
                    tbuf[(nm, par)] = Buf()

            def post(oc, tq, ps_i, par):
                dstd, row0, mode = outmap[oc]
                c0 = tq * TQW
                if mode == "plain":
                    t = tiles[("OT", par)]
                    tb = tbuf[("OT", par)]
                    ACT(t, PS[ps_i][:, :], AF.Copy, reads=[psbuf[ps_i]], writes=[tb])
                    ST(dstd[row0:row0 + 128, c0:c0 + TQW], t, reads=[tb],
                       writes=[B(src_name + "dst")])
                    return
                t = tiles[("TT", par)]; tb = tbuf[("TT", par)]
                r = tiles[("RT", par)]; rb = tbuf[("RT", par)]
                o1 = tiles[("O1", par)]; o1b = tbuf[("O1", par)]
                o2 = tiles[("O2", par)]; o2b = tbuf[("O2", par)]
                ot = tiles[("OT", par)]; otb = tbuf[("OT", par)]
                ACT(t, PS[ps_i][:, :], AF.Copy, reads=[psbuf[ps_i]], writes=[tb])
                ACT(r[0:64, :], t[64:128, :], AF.Copy, reads=[tb], writes=[rb], scale=-1.0)
                ACT(r[64:128, :], t[0:64, :], AF.Copy, reads=[tb], writes=[rb])
                TT(o1, t, RCOS[:, c0:c0 + TQW], ALU.mult, reads=[tb, consts_buf], writes=[o1b])
                TT(o2, r, RSIN[:, c0:c0 + TQW], ALU.mult, reads=[rb, consts_buf], writes=[o2b])
                TT(ot, o1, o2, ALU.add, reads=[o1b, o2b], writes=[otb])
                if mode == "ropelu":
                    r1 = tiles[("R1", par)]; r1b = tbuf[("R1", par)]
                    db = tiles[("DB", par)]; dbb = tbuf[("DB", par)]
                    eb = tiles[("EB", par)]; ebb = tbuf[("EB", par)]
                    ACT(r1, ot, AF.Relu, reads=[otb], writes=[r1b])
                    TT(db, ot, r1, ALU.subtract, reads=[otb, r1b], writes=[dbb])
                    ACT(eb, db, AF.Exp, reads=[dbb], writes=[ebb])
                    TT(ot, r1, eb, ALU.add, reads=[r1b, ebb], writes=[otb])
                ST(dstd[row0:row0 + 128, c0:c0 + TQW], ot, reads=[otb],
                   writes=[B(src_name + "dst")])

            for og in range(0, n_oc, OG):
                for s in range(OG):
                    oc = og + s
                    for cg in range(2):
                        LD3(WST[oc % 2][:, cg * 1024:(cg + 1) * 1024],
                            wT[cg * 1024:(cg + 1) * 1024, oc * 128:(oc + 1) * 128],
                            128, reads=[], writes=[wstb[oc % 2]])
                    ACT(WGB[:, s * 2048:(s + 1) * 2048], WST[oc % 2], AF.Copy,
                        reads=[wstb[oc % 2]], writes=[wgb[s]])
                for tq in range(NTQ):
                    c0 = tq * TQW
                    sl = tq % 2
                    for cg in range(2):
                        LD3(SRCS[sl][:, cg * 4096:(cg + 1) * 4096],
                            src_dram[cg * 1024:(cg + 1) * 1024, c0:c0 + TQW],
                            TQW, reads=[B(src_name)], writes=[srcb[sl]])
                    for s in range(OG):
                        oc = og + s
                        ps_i = (tq * OG + s) % 2
                        for c in range(NCH):
                            MM(PS[ps_i][:, :], WGB[:, (s * 16 + c) * 128:(s * 16 + c + 1) * 128],
                               SRCS[sl][:, c * 512:(c + 1) * 512],
                               c == 0, c == NCH - 1, [wgb[s], srcb[sl]], psbuf[ps_i])
                        post(oc, tq, ps_i, ps_i)
            p.barrier()

        def proj_res(wT, src_dram, src_name, out_dram, out_name, extra=False):
            OG = 8
            WGB = ARB[:, PB0:PB0 + OG * 2048]
            SRCS = [ARB[:, PB0 + 16384 + i * 8192:PB0 + 16384 + (i + 1) * 8192]
                    for i in range(2)]
            wgb = [Buf() for _ in range(OG)]
            srcb = [Buf(), Buf()]
            WST = [ARF[:, PF0 + i * 2048:PF0 + (i + 1) * 2048] for i in range(2)]
            wstb = [Buf(), Buf()]
            RT = [ARF[:, PF0 + 4096 + i * 512:PF0 + 4096 + (i + 1) * 512] for i in range(2)]
            SF = [ARF[:, PF0 + 5120 + i * 512:PF0 + 5120 + (i + 1) * 512] for i in range(2)]
            XE = [ARF[:, PF0 + 6144 + i * 512:PF0 + 6144 + (i + 1) * 512] for i in range(2)]
            rtb = [Buf(), Buf()]
            sfb = [Buf(), Buf()]
            xeb = [Buf(), Buf()]
            for og in range(0, 16, OG):
                for s in range(OG):
                    dc = og + s
                    for cg in range(2):
                        LD3(WST[dc % 2][:, cg * 1024:(cg + 1) * 1024],
                            wT[cg * 1024:(cg + 1) * 1024, dc * 128:(dc + 1) * 128],
                            128, reads=[], writes=[wstb[dc % 2]])
                    ACT(WGB[:, s * 2048:(s + 1) * 2048], WST[dc % 2], AF.Copy,
                        reads=[wstb[dc % 2]], writes=[wgb[s]])
                for tq in range(NTQ):
                    c0 = tq * TQW
                    sl = tq % 2
                    for cg in range(2):
                        LD3(SRCS[sl][:, cg * 4096:(cg + 1) * 4096],
                            src_dram[cg * 1024:(cg + 1) * 1024, c0:c0 + TQW],
                            TQW, reads=[B(src_name)], writes=[srcb[sl]])
                    for s in range(OG):
                        dc = og + s
                        ps_i = (tq * OG + s) % 2
                        for c in range(NCH):
                            MM(PS[ps_i][:, :], WGB[:, (s * 16 + c) * 128:(s * 16 + c + 1) * 128],
                               SRCS[sl][:, c * 512:(c + 1) * 512],
                               c == 0, c == NCH - 1, [wgb[s], srcb[sl]], psbuf[ps_i])
                        par = ps_i
                        LD(RT[par], R[dc * 128:(dc + 1) * 128, c0:c0 + TQW],
                           reads=rtiles("R", dc, c0, TQW), writes=[rtb[par]])
                        TT(SF[par], RT[par], PS[ps_i][:, :], ALU.add,
                           reads=[rtb[par], psbuf[ps_i]], writes=[sfb[par]])
                        if extra:
                            LD(XE[par], XN[dc * 128:(dc + 1) * 128, c0:c0 + TQW],
                               reads=[B("XN")], writes=[xeb[par]])
                            TT(SF[par], SF[par], XE[par], ALU.add,
                               reads=[sfb[par], xeb[par]], writes=[sfb[par]])
                        ST(out_dram[dc * 128:(dc + 1) * 128, c0:c0 + TQW], SF[par],
                           reads=[sfb[par]], writes=rtiles(out_name, dc, c0, TQW))
            p.barrier()

        def attn(causal):
            QH = ARB[:, PB0:PB0 + S]
            KH = ARB[:, PB0 + S:PB0 + 2 * S]
            VH = ARB[:, PB0 + 2 * S:PB0 + 3 * S]
            VT = ARB[:, PB0 + 3 * S:PB0 + 4 * S]
            PT = [ARB[:, PB0 + 4 * S + i * 512:PB0 + 4 * S + (i + 1) * 512] for i in range(2)]
            MT = [ARB[:, PB0 + 4 * S + 1024 + i * 512:PB0 + 4 * S + 1024 + (i + 1) * 512]
                  for i in range(2)]
            AOT = ARB[:, PB0 + 4 * S + 2048:PB0 + 4 * S + 2560]
            OF = ARF[:, PF0:PF0 + 512]
            RD = ARF[0:1, PF0 + 512:PF0 + 1024]
            qhb, khb, vhb, vtb = Buf(), Buf(), Buf(), Buf()
            ptb = [Buf(), Buf()]
            mtb = [Buf(), Buf()]
            aob, ofb, rdb = Buf(), Buf(), Buf()
            for h in range(NH):
                r0 = h * 128
                LD(QH, QB[r0:r0 + 128, :], reads=[B("QBd")], writes=[qhb])
                LD(KH, KB[r0:r0 + 128, :], reads=[B("KBd")], writes=[khb])
                LD(VH, VB[r0:r0 + 128, :], reads=[B("VBd")], writes=[vhb])
                for j in range(32):
                    pj = j % 2
                    psb_t = (PSB, psbbuf[0]) if pj == 0 else (PSB2, psbbuf[1])
                    TRANS(psb_t[0][:, 0:128], VH[:, j * 128:(j + 1) * 128],
                          [vhb, consts_buf], psb_t[1])
                    ACT(VT[:, j * 128:(j + 1) * 128], psb_t[0][:, 0:128], AF.Copy,
                        reads=[psb_t[1]], writes=[vtb])
                for tq in range(NTQ):
                    c0 = tq * TQW
                    jmax = (4 * tq + 4) if causal else 32
                    for j in range(jmax):
                        ps_i = j % 2
                        MM(PS[ps_i][:, :], KH[:, j * 128:(j + 1) * 128], QH[:, c0:c0 + TQW],
                           True, True, [khb, qhb], psbuf[ps_i])
                        ACT(PT[ps_i], PS[ps_i][:, :], AF.Exp, reads=[psbuf[ps_i]],
                            writes=[ptb[ps_i]], scale=ISQ)
                        src_t, src_b = PT[ps_i], ptb[ps_i]
                        if causal and j >= 4 * tq:
                            sdiag = j - 4 * tq
                            TT(MT[ps_i], PT[ps_i], DM[:, sdiag * 512:(sdiag + 1) * 512],
                               ALU.mult, reads=[ptb[ps_i], consts_buf], writes=[mtb[ps_i]])
                            src_t, src_b = MT[ps_i], mtb[ps_i]
                        MM(PS[2][:, :], VT[:, j * 128:(j + 1) * 128], src_t,
                           j == 0, j == jmax - 1, [vtb, src_b], psbuf[2])
                        MM(PS[3][0:1, :], ONESB, src_t,
                           j == 0, j == jmax - 1, [consts_buf, src_b], psbuf[3])
                    RECIP(RD, PS[3][0:1, :], reads=[psbuf[3]], writes=[rdb])
                    MM(PS[4][:, :], ONES1R, RD, True, True, [consts_buf, rdb], psbuf[4])
                    ACT(OF, PS[2][:, :], AF.Copy, reads=[psbuf[2]], writes=[ofb])
                    TT(OF, OF, PS[4][:, :], ALU.mult, reads=[ofb, psbuf[4]], writes=[ofb])
                    ACT(AOT, OF, AF.Copy, reads=[ofb], writes=[aob])
                    ST(AOB[r0:r0 + 128, c0:c0 + TQW], AOT, reads=[aob], writes=[B("AOBd")])
            p.barrier()

        def mod_phase():
            SRCS = [ARB[:, PB0 + i * 8192:PB0 + (i + 1) * 8192] for i in range(2)]
            srcb = [Buf(), Buf()]
            TF = ARF[0:16, PF0:PF0 + 512]
            UF = ARF[0:16, PF0 + 512:PF0 + 1024]
            tfb, ufb = Buf(), Buf()
            for tq in range(NTQ):
                c0 = tq * TQW
                for cg in range(2):
                    LD3(SRCS[0][:, cg * 4096:(cg + 1) * 4096],
                        ATTB[cg * 1024:(cg + 1) * 1024, c0:c0 + TQW],
                        TQW, reads=[B("ATTBd")], writes=[srcb[0]])
                    LD3(SRCS[1][:, cg * 4096:(cg + 1) * 4096],
                        HB2[cg * 1024:(cg + 1) * 1024, c0:c0 + TQW],
                        TQW, reads=[B("HB2")], writes=[srcb[1]])
                for c in range(NCH):
                    MM(PS[0][0:16, :], FAB[:, c * 16:(c + 1) * 16],
                       SRCS[0][:, c * 512:(c + 1) * 512],
                       c == 0, c == NCH - 1, [consts_buf, srcb[0]], psbuf[0])
                    MM(PS[1][0:16, :], FAB[:, c * 16:(c + 1) * 16],
                       SRCS[1][:, c * 512:(c + 1) * 512],
                       c == 0, c == NCH - 1, [consts_buf, srcb[1]], psbuf[1])
                ACT(TF, PS[0][0:16, :], AF.Tanh, reads=[psbuf[0]], writes=[tfb])
                TT(UF, TF, PS[1][0:16, :], ALU.mult, reads=[tfb, psbuf[1]], writes=[ufb])
                ACT(MHA[:, c0:c0 + TQW], UF, AF.Copy, reads=[ufb], writes=[mha_buf],
                    scale=0.1)
            p.barrier()

        def mlp_gateup():
            HBH = ARB[:, PB0:PB0 + 32768]
            WIB = ARB[:, PB0 + 32768:PB0 + 34816]
            WUB = ARB[:, PB0 + 34816:PB0 + 36864]
            GT = [ARB[:, PB0 + 36864 + i * 512:PB0 + 36864 + (i + 1) * 512] for i in range(2)]
            UT = [ARB[:, PB0 + 37888 + i * 512:PB0 + 37888 + (i + 1) * 512] for i in range(2)]
            PTt = [ARB[:, PB0 + 38912 + i * 512:PB0 + 38912 + (i + 1) * 512] for i in range(2)]
            WST = [ARF[:, PF0 + i * 2048:PF0 + (i + 1) * 2048] for i in range(2)]
            hbb = Buf()
            wib, wub = Buf(), Buf()
            wstb = [Buf(), Buf()]
            gtb = [Buf(), Buf()]
            utb = [Buf(), Buf()]
            ptb = [Buf(), Buf()]
            for half in range(2):
                h0 = half * 2048
                for c in range(NCH):
                    LD(HBH[:, c * 2048:(c + 1) * 2048],
                       HB2[c * 128:(c + 1) * 128, h0:h0 + 2048],
                       reads=[B("HB2")], writes=[hbb])
                for fc in range(64):
                    for cg in range(2):
                        LD3(WST[0][:, cg * 1024:(cg + 1) * 1024],
                            fwinT[cg * 1024:(cg + 1) * 1024, fc * 128:(fc + 1) * 128],
                            128, reads=[], writes=[wstb[0]])
                        LD3(WST[1][:, cg * 1024:(cg + 1) * 1024],
                            fwupT[cg * 1024:(cg + 1) * 1024, fc * 128:(fc + 1) * 128],
                            128, reads=[], writes=[wstb[1]])
                    ACT(WIB, WST[0], AF.Copy, reads=[wstb[0]], writes=[wib])
                    ACT(WUB, WST[1], AF.Copy, reads=[wstb[1]], writes=[wub])
                    for tql in range(4):
                        t0 = tql * 512
                        pg = tql % 2
                        for c in range(NCH):
                            MM(PS[pg][:, :], WIB[:, c * 128:(c + 1) * 128],
                               HBH[:, c * 2048 + t0:c * 2048 + t0 + 512],
                               c == 0, False, [wib, hbb], psbuf[pg])
                        MM(PS[pg][:, :], FBB[:, fc * 128:(fc + 1) * 128],
                           MHA[:, h0 + t0:h0 + t0 + 512],
                           False, True, [consts_buf, mha_buf], psbuf[pg])
                        ACT(GT[pg], PS[pg][:, :], AF.Silu, reads=[psbuf[pg]], writes=[gtb[pg]])
                        for c in range(NCH):
                            MM(PS[2 + pg][:, :], WUB[:, c * 128:(c + 1) * 128],
                               HBH[:, c * 2048 + t0:c * 2048 + t0 + 512],
                               c == 0, c == NCH - 1, [wub, hbb], psbuf[2 + pg])
                        ACT(UT[pg], PS[2 + pg][:, :], AF.Copy, reads=[psbuf[2 + pg]],
                            writes=[utb[pg]])
                        TT(PTt[pg], GT[pg], UT[pg], ALU.mult,
                           reads=[gtb[pg], utb[pg]], writes=[ptb[pg]])
                        ST(PRD[fc * 128:(fc + 1) * 128, h0 + t0:h0 + t0 + 512], PTt[pg],
                           reads=[ptb[pg]], writes=[B("PRDd")])
            p.barrier()

        def mlp_down(out_dram, out_name):
            WDB = ARB[:, PB0:PB0 + 32768]
            PRS = [ARB[:, PB0 + 32768 + i * 8192:PB0 + 32768 + (i + 1) * 8192]
                   for i in range(2)]
            WST = ARF[:, PF0:PF0 + 2048]
            RT = [ARF[:, PF0 + 2048 + i * 128:PF0 + 2048 + (i + 1) * 128] for i in range(2)]
            XE = [ARF[:, PF0 + 2304 + i * 128:PF0 + 2304 + (i + 1) * 128] for i in range(2)]
            SF = [ARF[:, PF0 + 2560 + i * 128:PF0 + 2560 + (i + 1) * 128] for i in range(2)]
            wdb = Buf()
            prsb = [Buf(), Buf()]
            wstb = Buf()
            rtb = [Buf(), Buf()]
            xeb = [Buf(), Buf()]
            sfb = [Buf(), Buf()]
            for dg in range(4):
                d0 = dg * 512
                for fg in range(16):
                    for cg in range(2):
                        LD3(WST[:, cg * 1024:(cg + 1) * 1024],
                            fwdT[fg * 512 + cg * 256:fg * 512 + (cg + 1) * 256,
                                 d0:d0 + 512],
                            512, reads=[], writes=[wstb])
                    ACT(WDB[:, fg * 2048:(fg + 1) * 2048], WST, AF.Copy,
                        reads=[wstb], writes=[wdb])
                for ts in range(32):
                    t0 = ts * 128
                    sl = ts % 2
                    for fg in range(8):
                        LD3(PRS[sl][:, fg * 1024:(fg + 1) * 1024],
                            PRD[fg * 1024:(fg + 1) * 1024, t0:t0 + 128],
                            128, reads=[B("PRDd")], writes=[prsb[sl]])
                    for dl in range(4):
                        dc = dg * 4 + dl
                        par = dl % 2
                        for fc in range(64):
                            MM(PS[dl][:, 0:128],
                               WDB[:, fc * 512 + dl * 128:fc * 512 + (dl + 1) * 128],
                               PRS[sl][:, fc * 128:(fc + 1) * 128],
                               fc == 0, fc == 63, [wdb, prsb[sl]], psbuf[dl])
                        LD(RT[par], R[dc * 128:(dc + 1) * 128, t0:t0 + 128],
                           reads=rtiles("R", dc, t0, 128), writes=[rtb[par]])
                        LD(XE[par], XN[dc * 128:(dc + 1) * 128, t0:t0 + 128],
                           reads=[B("XN")], writes=[xeb[par]])
                        TT(SF[par], RT[par], PS[dl][:, 0:128], ALU.add,
                           reads=[rtb[par], psbuf[dl]], writes=[sfb[par]])
                        TT(SF[par], SF[par], XE[par], ALU.add,
                           reads=[sfb[par], xeb[par]], writes=[sfb[par]])
                        ST(out_dram[dc * 128:(dc + 1) * 128, t0:t0 + 128], SF[par],
                           reads=[sfb[par]], writes=rtiles(out_name, dc, t0, 128))
            p.barrier()

        def linattn():
            KFH = ARB[:, PB0:PB0 + S]
            VH = ARB[:, PB0 + S:PB0 + 2 * S]
            KFT = ARB[:, PB0 + 2 * S:PB0 + 3 * S]
            VT = ARB[:, PB0 + 3 * S:PB0 + 4 * S]
            SST = ARB[:, PB0 + 4 * S:PB0 + 5 * S]
            ZST = ARB[:, PB0 + 5 * S:PB0 + 5 * S + 32]
            QFH = ARB[:, PB0 + 5 * S + 64:PB0 + 6 * S + 64]
            PTl = ARB[:, PB0 + 6 * S + 64:PB0 + 6 * S + 192]
            MTl = ARB[:, PB0 + 6 * S + 192:PB0 + 6 * S + 320]
            AOT = ARB[:, PB0 + 6 * S + 320:PB0 + 6 * S + 448]
            NF = ARF[:, PF0:PF0 + 128]
            DF = ARF[0:1, PF0 + 128:PF0 + 256]
            kfb, vhb, kftb, vtb, sstb, zstb, qfb = (Buf() for _ in range(7))
            ptb, mtb, aob, nfb, dfb = (Buf() for _ in range(5))
            for g in range(4):
                r0 = g * 128
                LD(KFH, KF2[r0:r0 + 128, :], reads=[B("KF2d")], writes=[kfb])
                LD(VH, VB2[r0:r0 + 128, :], reads=[B("VB2d")], writes=[vhb])
                for j in range(32):
                    pj = j % 2
                    psb_t = (PSB, psbbuf[0]) if pj == 0 else (PSB2, psbbuf[1])
                    TRANS(psb_t[0][:, 0:128], KFH[:, j * 128:(j + 1) * 128],
                          [kfb, consts_buf], psb_t[1])
                    ACT(KFT[:, j * 128:(j + 1) * 128], psb_t[0][:, 0:128], AF.Copy,
                        reads=[psb_t[1]], writes=[kftb])
                    TRANS(psb_t[0][:, 128:256], VH[:, j * 128:(j + 1) * 128],
                          [vhb, consts_buf], psb_t[1])
                    ACT(VT[:, j * 128:(j + 1) * 128], psb_t[0][:, 128:256], AF.Copy,
                        reads=[psb_t[1]], writes=[vtb])
                for c in range(32):
                    if c == 0:
                        p.emit("dve", lambda E: E.memset(SST[:, 0:128], 0.0),
                               reads=[], writes=[sstb])
                        p.emit("dve", lambda E: E.memset(ZST[:, 0:1], 0.0),
                               reads=[], writes=[zstb])
                    else:
                        ACT(SST[:, c * 128:(c + 1) * 128], PS[0][:, 0:128], AF.Copy,
                            reads=[psbuf[0]], writes=[sstb])
                        ACT(ZST[:, c:c + 1], PS[1][:, 0:1], AF.Copy,
                            reads=[psbuf[1]], writes=[zstb])
                    MM(PS[0][:, 0:128], KFT[:, c * 128:(c + 1) * 128],
                       VT[:, c * 128:(c + 1) * 128],
                       c == 0, c == 31, [kftb, vtb], psbuf[0])
                    MM(PS[1][:, 0:1], KFT[:, c * 128:(c + 1) * 128], ONESB,
                       c == 0, c == 31, [kftb, consts_buf], psbuf[1])
                for qi in range(4):
                    qh = g * 4 + qi
                    LD(QFH, QF2[qh * 128:(qh + 1) * 128, :], reads=[B("QF2d")], writes=[qfb])
                    for c in range(32):
                        t0 = c * 128
                        MM(PS[2][:, 0:128], KFH[:, t0:t0 + 128], QFH[:, t0:t0 + 128],
                           True, True, [kfb, qfb], psbuf[2])
                        ACT(PTl, PS[2][:, 0:128], AF.Copy, reads=[psbuf[2]], writes=[ptb])
                        TT(MTl, PTl, TRILB, ALU.mult, reads=[ptb, consts_buf], writes=[mtb])
                        MM(PS[3][:, 0:128], VT[:, t0:t0 + 128], MTl,
                           True, False, [vtb, mtb], psbuf[3])
                        MM(PS[3][:, 0:128], SST[:, t0:t0 + 128], QFH[:, t0:t0 + 128],
                           False, True, [sstb, qfb], psbuf[3])
                        MM(PS[4][0:1, 0:128], ONESB, MTl,
                           True, False, [consts_buf, mtb], psbuf[4])
                        MM(PS[4][0:1, 0:128], ZST[:, c:c + 1], QFH[:, t0:t0 + 128],
                           False, True, [zstb, qfb], psbuf[4])
                        ACT(DF, PS[4][0:1, 0:128], AF.Copy, reads=[psbuf[4]], writes=[dfb],
                            bias=1e-6)
                        RECIP(DF, DF, reads=[dfb], writes=[dfb])
                        MM(PS[5][:, 0:128], ONES1R, DF, True, True,
                           [consts_buf, dfb], psbuf[5])
                        ACT(NF, PS[3][:, 0:128], AF.Copy, reads=[psbuf[3]], writes=[nfb])
                        TT(NF, NF, PS[5][:, 0:128], ALU.mult,
                           reads=[nfb, psbuf[5]], writes=[nfb])
                        ACT(AOT, NF, AF.Copy, reads=[nfb], writes=[aob])
                        ST(AOB[qh * 128:(qh + 1) * 128, t0:t0 + 128], AOT,
                           reads=[aob], writes=[B("AOBd")])
            p.barrier()

        ln_phase(xt, False, XN, XNB, True)
        proj(wqkvT, 48, XNB, "XNB",
             {**{oc: (QB, oc * 128, "rope") for oc in range(16)},
              **{oc: (KB, (oc - 16) * 128, "rope") for oc in range(16, 32)},
              **{oc: (VB, (oc - 32) * 128, "plain") for oc in range(32, 48)}})
        attn(causal=False)
        proj_res(waoT, AOB, "AOBd", R, "R")
        ln_phase(R, True, XN, XNB, False)

        def ffn_block(final):
            rms_phase()
            proj(fwqT, 16, X1B, "X1B", {oc: (QB, oc * 128, "plain") for oc in range(16)})
            proj(fwkT, 16, X1B, "X1B", {oc: (KB, oc * 128, "plain") for oc in range(16)})
            proj(fwvT, 16, X1B, "X1B", {oc: (VB, oc * 128, "plain") for oc in range(16)})
            attn(causal=True)
            proj(fwoT, 16, AOB, "AOBd", {oc: (ATTB, oc * 128, "plain") for oc in range(16)})
            mod_phase()
            mlp_gateup()
            if final:
                mlp_down(OUT, "out")
            else:
                mlp_down(R, "R")

        ffn_block(final=False)
        ln_phase(R, True, XN, XNB, False)
        proj(bwqT, 16, XNB, "XNB", {oc: (QF2, oc * 128, "ropelu") for oc in range(16)})
        proj(bwkT, 4, XNB, "XNB", {oc: (KF2, oc * 128, "ropelu") for oc in range(4)})
        proj(bwvT, 4, XNB, "XNB", {oc: (VB2, oc * 128, "plain") for oc in range(4)})
        linattn()
        proj_res(bwoT, AOB, "AOBd", R, "R")
        ln_phase(R, True, XN, XNB, False)
        ffn_block(final=True)

        fin = p.cnt["pool"]
        p.streams["pool"].append(("w", "pool", fin))
        p.barrier()

        def replay(key):
            def go(eng):
                for item in p.streams[key]:
                    if item[0] == "w":
                        eng.wait_ge(sems[item[1]], item[2])
                    else:
                        item[1](eng).then_inc(sems[key], item[2])
            return go

        with nc.Block() as blk:
            blk.sync(replay("sp"))
            blk.tensor(replay("pe"))
            blk.scalar(replay("act"))
            blk.vector(replay("dve"))
            blk.gpsimd(replay("pool"))

    return nc


_NC_CACHE = None


def kernel(**inputs):
    global _NC_CACHE
    x = inputs["L0_attn_a_x"]

    def T(a):
        return np.ascontiguousarray(np.asarray(a).T.astype(np.float32))

    base = {
        "wqkvT": T(inputs["L0_attn_a_w_qkv"]),
        "waoT": T(inputs["L0_attn_a_w_out"]),
        "fwqT": T(inputs["L0_ffn_Wq"]),
        "fwkT": T(inputs["L0_ffn_Wk"]),
        "fwvT": T(inputs["L0_ffn_Wv"]),
        "fwoT": T(inputs["L0_ffn_Wo"]),
        "fA": np.ascontiguousarray(np.asarray(inputs["L0_ffn_A"], dtype=np.float32)),
        "fB": np.ascontiguousarray(np.asarray(inputs["L0_ffn_B"], dtype=np.float32)),
        "fwinT": T(inputs["L0_ffn_W_in_base"]),
        "fwupT": T(inputs["L0_ffn_W_up_base"]),
        "fwdT": T(inputs["L0_ffn_W_down"]),
        "bwqT": T(inputs["L0_attn_b_Wq"]),
        "bwkT": T(inputs["L0_attn_b_Wk"]),
        "bwvT": T(inputs["L0_attn_b_Wv"]),
        "bwoT": T(inputs["L0_attn_b_Wo"]),
        "ln1": np.ascontiguousarray(
            np.asarray(inputs["L0_ffn_ln1_w"], dtype=np.float32).reshape(16, 128).T),
        "ln2": np.ascontiguousarray(
            np.asarray(inputs["L0_ffn_ln2_w"], dtype=np.float32).reshape(16, 128).T),
    }

    inv = 1.0 / (10000.0 ** (np.arange(0, HD, 2, dtype=np.float32) / HD))
    t = np.arange(S, dtype=np.float32)
    fr = np.outer(t, inv)
    emb = np.concatenate([fr, fr], axis=1)
    base["ropec"] = np.ascontiguousarray(np.cos(emb).T.astype(np.float32))
    base["ropes"] = np.ascontiguousarray(np.sin(emb).T.astype(np.float32))
    kk = np.arange(128)[:, None]
    qq = np.arange(512)[None, :]
    dm = []
    for s in range(4):
        dm.append((s * 128 + kk <= qq).astype(np.float32))
    base["dmask"] = np.ascontiguousarray(np.concatenate(dm, axis=1))
    base["tril"] = np.ascontiguousarray(
        (np.arange(128)[:, None] <= np.arange(128)[None, :]).astype(np.float32))
    base["iden"] = np.eye(128, dtype=np.float32)
    base["onescol"] = np.ones((128, 1), dtype=np.float32)
    base["onesrow"] = np.ones((1, 128), dtype=np.float32)

    if _NC_CACHE is None:
        _NC_CACHE = build_program()
    nc = _NC_CACHE

    in_maps = []
    for core in range(8):
        b = core % 4
        m = dict(base)
        m["xt"] = np.ascontiguousarray(np.asarray(x[b], dtype=np.float32).T)
        in_maps.append(m)

    res = run_bass_kernel_spmd(nc, in_maps, core_ids=list(range(8)))
    out = np.empty((4, S, D), dtype=np.float32)
    for b in range(4):
        out[b] = res.results[b]["out"].T
    return out
